# version 01
# baseline (speedup 1.0000x reference)
import numpy as np
import ml_dtypes
import concourse.bass as bass
import concourse.mybir as mybir
from concourse.bass_utils import run_bass_kernel_spmd

F32 = mybir.dt.float32
BF16 = mybir.dt.bfloat16
U8 = mybir.dt.uint8
AF = mybir.ActivationFunctionType
ALU = mybir.AluOpType
AX = mybir.AxisListType

ENGINES = ("sync", "gpsimd", "tensor", "vector", "scalar")

T = 2048
C = 1024
NT = T // 128
NC = C // 128
HD = 64
NHEAD = 16
HMOE = 1408
NHK = HMOE // 128
NEXP = 9
NCORE = 8
GTOK = NCORE * T
KB = 1024


class Sem:
    __slots__ = ("h", "n", "name")

    def __init__(self, nc, name):
        self.h = nc.alloc_semaphore(name)
        self.n = 0
        self.name = name


class Prog:
    def __init__(self, nc):
        self.nc = nc
        self.q = {e: [] for e in ENGINES}
        self._sems = {}

    def sem(self, name):
        if name not in self._sems:
            self._sems[name] = Sem(self.nc, name)
        return self._sems[name]

    def add(self, eng, fn, inc=None, k=1):
        if inc is not None:
            inc.n += k
            self.q[eng].append(lambda e, fn=fn, s=inc.h, k=k: fn(e).then_inc(s, k))
            return inc.n
        self.q[eng].append(fn)
        return None

    def wait(self, eng, sem, n):
        if n and n > 0:
            self.q[eng].append(lambda e, s=sem.h, n=n: e.wait_ge(s, n))

    def addf(self, eng, fn):
        sem = self.sem("flush_" + eng)
        n = self.add(eng, fn, inc=sem)
        self.wait(eng, sem, n)
        return n

    def finalize(self):
        with self.nc.Block() as block:
            block.sync(lambda e: [f(e) for f in self.q["sync"]])
            block.gpsimd(lambda e: [f(e) for f in self.q["gpsimd"]])
            block.tensor(lambda e: [f(e) for f in self.q["tensor"]])
            block.vector(lambda e: [f(e) for f in self.q["vector"]])
            block.scalar(lambda e: [f(e) for f in self.q["scalar"]])


def _dt_size(dt):
    return {F32: 4, BF16: 2, U8: 1}[dt]



def _cm(W):
    Cin, Cout = np.asarray(W).shape
    return np.ascontiguousarray(
        np.asarray(W, np.float32).reshape(Cin // 128, 128, Cout)
        .transpose(1, 0, 2).reshape(128, -1))


def _pp(v):
    v = np.asarray(v, dtype=np.float32)
    K = v.shape[0] // 128
    return np.ascontiguousarray(v.reshape(K, 128).T)


def make_host_globals(inputs):
    g = {}
    g["w_qkv"] = _cm(inputs["L0_attn_Wqkv"])
    g["b_qkv_row"] = np.asarray(inputs["L0_attn_bqkv"], np.float32).reshape(1, 3 * C)
    g["w_proj"] = _cm(inputs["L0_attn_Wproj"])
    g["b_proj_pp"] = _pp(inputs["L0_attn_bproj"])
    g["w_r"] = _cm(np.asarray(inputs["L0_ffn_W_r"], np.float32).T)
    g["w_k"] = _cm(np.asarray(inputs["L0_ffn_W_k"], np.float32).T)
    g["w_v"] = _cm(np.asarray(inputs["L0_ffn_W_v"], np.float32).T)
    g["mu_pp"] = np.ascontiguousarray(
        np.concatenate([_pp(inputs["L0_ffn_mu_r"]), _pp(inputs["L0_ffn_mu_k"])],
                       axis=1))
    g["w_q1"] = _cm(np.asarray(inputs["L1_attn_Wq"], np.float32).T)
    g["w_k1"] = _cm(np.asarray(inputs["L1_attn_Wk"], np.float32).T)
    g["w_v1"] = _cm(np.asarray(inputs["L1_attn_Wv"], np.float32).T)
    g["b_qkv1_pp"] = np.ascontiguousarray(
        np.stack([np.asarray(inputs["L1_attn_bq"], np.float32),
                  np.asarray(inputs["L1_attn_bk"], np.float32),
                  np.asarray(inputs["L1_attn_bv"], np.float32)], axis=1))
    g["w_gate"] = _cm(np.asarray(inputs["L1_ffn_gate_w"], np.float32).T)
    g["b_gate_row"] = np.asarray(inputs["L1_ffn_gate_b"], np.float32).reshape(1, 8)

    w1 = np.asarray(inputs["L1_ffn_w1"], np.float32)
    w3 = np.asarray(inputs["L1_ffn_w3"], np.float32)
    w2 = np.asarray(inputs["L1_ffn_w2"], np.float32)
    sw1 = np.asarray(inputs["L1_ffn_sw1"], np.float32)
    sw3 = np.asarray(inputs["L1_ffn_sw3"], np.float32)
    sw2 = np.asarray(inputs["L1_ffn_sw2"], np.float32)

    def blocks13(w_all, sw):
        out = np.empty((NEXP * NHK, 128, NC * 128), np.float32)
        for e in range(NEXP):
            w = sw if e == 8 else w_all[e]
            wr = w.reshape(NC, 128, NHK, 128)
            for hk in range(NHK):
                out[e * NHK + hk] = wr[:, :, hk, :].transpose(1, 0, 2).reshape(128, -1)
        return np.ascontiguousarray(out)

    g["w1b"] = blocks13(w1, sw1)
    g["w3b"] = blocks13(w3, sw3)
    w2b = np.empty((NEXP * NHK, 128, C), np.float32)
    for e in range(NEXP):
        w2b[e * NHK:(e + 1) * NHK] = (sw2 if e == 8 else w2[e]).reshape(NHK, 128, C)
    g["w2b"] = np.ascontiguousarray(w2b)

    def b13(b_all, sb):
        cols = [(sb if e == 8 else b_all[e]).reshape(NHK, 128).T for e in range(NEXP)]
        return np.ascontiguousarray(np.concatenate(cols, axis=1), dtype=np.float32)

    g["b1_pp"] = b13(np.asarray(inputs["L1_ffn_b1"], np.float32),
                     np.asarray(inputs["L1_ffn_sb1"], np.float32))
    g["b3_pp"] = b13(np.asarray(inputs["L1_ffn_b3"], np.float32),
                     np.asarray(inputs["L1_ffn_sb3"], np.float32))
    g["b2cat"] = np.ascontiguousarray(
        np.concatenate([np.asarray(inputs["L1_ffn_b2"], np.float32),
                        np.asarray(inputs["L1_ffn_sb2"], np.float32).reshape(1, C)],
                       axis=0))

    g["ident_bf"] = np.eye(128, dtype=ml_dtypes.bfloat16)
    g["ident_f32"] = np.eye(128, dtype=np.float32)
    g["mask_bf"] = np.where(np.tril(np.ones((128, 128), bool)), 0.0, -1e9
                            ).astype(ml_dtypes.bfloat16)
    g["ones_bf"] = np.ones((1, 512), dtype=ml_dtypes.bfloat16)
    g["ones_f32"] = np.ones((1, 128), dtype=np.float32)
    return g


GLOBAL_SPECS = [
    ("w_qkv", (128, NC * 3 * C), F32), ("b_qkv_row", (1, 3 * C), F32),
    ("w_proj", (128, NC * C), F32), ("b_proj_pp", (128, NC), F32),
    ("w_r", (128, NC * C), F32), ("w_k", (128, NC * C), F32),
    ("w_v", (128, NC * C), F32), ("mu_pp", (128, 16), F32),
    ("w_q1", (128, NC * HD), F32), ("w_k1", (128, NC * HD), F32),
    ("w_v1", (128, NC * HD), F32), ("b_qkv1_pp", (64, 3), F32),
    ("w_gate", (128, NC * 8), F32), ("b_gate_row", (1, 8), F32),
    ("w1b", (NEXP * NHK, 128, NC * 128), F32),
    ("w3b", (NEXP * NHK, 128, NC * 128), F32),
    ("w2b", (NEXP * NHK, 128, C), F32),
    ("b1_pp", (128, NEXP * NHK), F32), ("b3_pp", (128, NEXP * NHK), F32),
    ("b2cat", (NEXP, C), F32),
    ("ident_bf", (128, 128), BF16), ("ident_f32", (128, 128), F32),
    ("mask_bf", (128, 128), BF16), ("ones_bf", (1, 512), BF16),
    ("ones_f32", (1, 128), F32),
]



class BuildA:
    def __init__(self, stop="all", dbg=()):
        self.stop = stop
        self.dbg = set(dbg)
        nc = bass.Bass("TRN2", target_bir_lowering=False)
        self.nc = nc
        self.P = Prog(nc)
        self.x_in = nc.dram_tensor("x", [T, C], F32, kind="ExternalInput").ap()
        self.g = {}
        for name, shape, dt in GLOBAL_SPECS:
            self.g[name] = nc.dram_tensor(name, list(shape), dt,
                                          kind="ExternalInput").ap()
        self.xo = nc.dram_tensor("xo", [T, C], F32, kind="ExternalOutput").ap()
        self.qo = nc.dram_tensor("qo", [HD, T], BF16, kind="ExternalOutput").ap()
        self.ko = nc.dram_tensor("ko", [HD, T], BF16, kind="ExternalOutput").ap()
        self.vo = nc.dram_tensor("vo", [128, NT * HD], BF16, kind="ExternalOutput").ap()

        sbt = nc.alloc_sbuf_tensor
        self.ident_bf = sbt("s_ident_bf", [128, 128], BF16).ap()
        self.ident_f32 = sbt("s_ident_f32", [128, 128], F32).ap()
        self.mask_bf = sbt("s_mask_bf", [128, 128], BF16).ap()
        self.ones_bf = sbt("s_ones_bf", [1, 512], BF16).ap()
        self.ones_f32 = sbt("s_ones_f32", [1, 128], F32).ap()
        self.bqkv_row = sbt("bqkv_row", [1, 3 * C], BF16).ap()
        self.bproj_pp = sbt("bproj_pp", [128, NC], F32).ap()
        self.mu_pp = sbt("s_mu_pp", [128, 16], F32).ap()
        self.bqkv1_pp = sbt("bqkv1_pp", [64, 3], F32).ap()
        self.bgate_row = sbt("bgate_row", [1, 8], F32).ap()
        self.b1_pp = sbt("s_b1_pp", [128, NEXP * NHK], F32).ap()
        self.b3_pp = sbt("s_b3_pp", [128, NEXP * NHK], F32).ap()
        self.b2cat = sbt("s_b2cat", [NEXP, C], BF16).ap()
        self.wgate = sbt("wgate", [128, NC * 8], BF16).ap()
        self.gbb = sbt("gbb", [128, 8], F32).ap()
        self.stat = sbt("stat", [128, 16], F32).ap()
        self.gsc = sbt("gsc", [128, 32], F32).ap()
        self.rw_tm = sbt("rw_tm", [128, NT * NEXP], F32).ap()
        self.rw_fm = sbt("rw_fm", [NEXP, T], BF16).ap()
        self.scr = sbt("scr", [128, 2048], F32).ap()
        self.px = sbt("px", [128, T + 1], F32).ap()
        self.xnt = sbt("xnt", [128, 2 * C], BF16).ap()
        self.xnb = sbt("xnb", [128, NC * T], BF16).ap()

        rem = nc.sbuf_bytes_remaining
        self.arena_bytes = (rem - 3 * KB) // 4 * 4
        assert self.arena_bytes >= 127 * KB, f"arena too small: {self.arena_bytes}"
        self.AR = sbt("arena", [128, self.arena_bytes], U8).ap()

        self.ps_s = nc.alloc_psum_tensor("ps_s", [128, 2048], F32)
        self.pbA = nc.alloc_psum_tensor("pbA", [128, 512], F32)
        self.pbB = nc.alloc_psum_tensor("pbB", [128, 512], F32)
        self.pbC = nc.alloc_psum_tensor("pbC", [128, 512], F32)
        self.pbD = nc.alloc_psum_tensor("pbD", [128, 512], F32)

        self._build()

    def av(self, off_b, parts, cols, dt):
        n = cols * _dt_size(dt)
        assert off_b % 4 == 0 and off_b + n <= self.arena_bytes, (off_b, n)
        return self.AR[0:parts, off_b:off_b + n].bitcast(dt)

    def psf(self, t, cols=512, base=0, parts=128, pbase=0):
        return t.ap()[pbase:pbase + parts, base:base + cols]

    def psb(self, t, cols, base=0, parts=128, pbase=0):
        return t.ap().bitcast(BF16)[pbase:pbase + parts, base:base + cols]

    def dbg_store(self, name, src_ap, shape, dt, waits):
        if name not in self.dbg:
            return
        od = self.nc.dram_tensor("dbg_" + name, list(shape), dt,
                                 kind="ExternalOutput").ap()
        P, st = self.P, self.P.sem("st")
        for s, n in waits:
            P.wait("sync", s, n)
        P.add("sync", lambda e, o=od, a=src_ap: e.dma_start(out=o, in_=a), inc=st, k=16)

    def _build(self):
        self._load_consts()
        self._ln(src_kind="dram", dram=self.x_in, ready=None)
        self.dbg_store("xnb0", self.xnb, (128, NC * T), BF16,
                       [(self.P.sem("tA"), self.P.sem("tA").n)])
        if self.stop == "ln":
            return self._finish()
        self._ffn()
        if self.stop == "ffn":
            return self._finish()
        self._attn()
        if self.stop == "attn":
            return self._finish()
        self._proj_resid()
        if self.stop == "resid":
            return self._finish()
        self._ln(src_kind="dram", dram=self.xo, ready=self.x1_tile_ready)
        self.dbg_store("xnb1", self.xnb, (128, NC * T), BF16,
                       [(self.P.sem("tA"), self.P.sem("tA").n)])
        if self.stop == "ln1":
            return self._finish()
        self._qkv1()
        if self.stop == "qkv1":
            return self._finish()
        self._gate()
        if self.stop == "gate":
            return self._finish()
        self._moe()
        return self._finish()

    def _finish(self):
        P = self.P
        for s in ("st", "qXo", "qQKV"):
            sem = P.sem(s)
            if sem.n:
                P.wait("sync", sem, sem.n)
        qM = P.sem("qM")
        if qM.n:
            P.wait("gpsimd", qM, qM.n)
        P.finalize()

    def _load_consts(self):
        P = self.P
        qS, qG, tT, tV = P.sem("qS"), P.sem("qG"), P.sem("tT"), P.sem("tV")
        for t, name in [
            (self.ident_bf, "ident_bf"), (self.ident_f32, "ident_f32"),
            (self.mask_bf, "mask_bf"), (self.ones_bf, "ones_bf"),
            (self.ones_f32, "ones_f32"), (self.bproj_pp, "b_proj_pp"),
            (self.mu_pp, "mu_pp"), (self.bqkv1_pp, "b_qkv1_pp"),
            (self.b1_pp, "b1_pp"), (self.b3_pp, "b3_pp"),
            (self.bgate_row, "b_gate_row"),
        ]:
            P.add("sync", lambda e, t=t, a=self.g[name]: e.dma_start(out=t, in_=a),
                  inc=qS, k=16)
        for t, name in [(self.bqkv_row, "b_qkv_row"), (self.b2cat, "b2cat"),
                        (self.wgate, "w_gate")]:
            P.add("gpsimd", lambda e, t=t, a=self.g[name]: e.dma_start(out=t, in_=a),
                  inc=qG, k=16)
        self.n_const_s = qS.n
        self.n_const_g = qG.n
        P.wait("tensor", qS, self.n_const_s)
        n = P.add("tensor", lambda e: e.matmul(
            self.psf(self.pbD, 8), self.ones_f32[0:1, :], self.bgate_row,
            start=True, stop=True), inc=tT)
        P.wait("vector", tT, n)
        self.gbb_done = P.add("vector", lambda e: e.tensor_copy(
            self.gbb, self.psf(self.pbD, 8)), inc=tV)

    def _ln(self, src_kind, dram, ready):
        P = self.P
        qS, tV, tA, tT = P.sem("qS"), P.sem("tV"), P.sem("tA"), P.sem("tT")
        P.wait("tensor", tV, tV.n)
        P.wait("tensor", tA, tA.n)
        st = self.stat
        ln_x = [self.scr[:, 0:1024], self.scr[:, 1024:2048]]
        jk = self.px[:, 0:1024]
        packs = [self.pbA, self.pbB]
        evac_done = [0, 0]
        xnt_free = [0, 0]
        dve_read = [tV.n, tV.n]
        for n in range(NT):
            par = n % 2
            so = 8 * par
            xt = ln_x[par]
            if dve_read[par]:
                P.wait("sync", tV, dve_read[par])
            if ready is not None:
                P.wait("sync", ready[n][0], ready[n][1])
            ldn = P.add("sync", lambda e, n=n, xt=xt, dram=dram: e.dma_start(
                out=xt, in_=dram[n * 128:(n + 1) * 128, :]), inc=qS, k=16)
            P.wait("vector", qS, ldn)
            P.add("vector", lambda e, xt=xt, so=so: e.scalar_tensor_tensor(
                jk, xt, 1.0, xt, ALU.mult, ALU.mult, accum_out=st[:, so + 1:so + 2]))
            P.addf("vector", lambda e, xt=xt, so=so: e.tensor_scalar(
                jk, xt, 1.0, 0.0, ALU.mult, ALU.add, accum_out=st[:, so:so + 1]))
            P.add("vector", lambda e, so=so: e.tensor_scalar(
                st[:, so + 2:so + 3], st[:, so:so + 1], 1.0 / C, None, ALU.mult))
            P.addf("vector", lambda e, so=so: e.tensor_scalar(
                st[:, so + 3:so + 4], st[:, so + 1:so + 2], 1.0 / C, 1e-5,
                ALU.mult, ALU.add))
            P.addf("vector", lambda e, so=so: e.scalar_tensor_tensor(
                st[:, so + 5:so + 6], st[:, so + 2:so + 3], -1.0,
                st[:, so + 2:so + 3], ALU.mult, ALU.mult))
            nvar = P.add("vector", lambda e, so=so: e.tensor_add(
                st[:, so + 3:so + 4], st[:, so + 3:so + 4], st[:, so + 5:so + 6]),
                inc=tV)
            P.wait("scalar", tV, nvar)
            P.addf("scalar", lambda e, so=so: e.activation(
                st[:, so + 5:so + 6], st[:, so + 3:so + 4], AF.Ln))
            nrstd = P.add("scalar", lambda e, so=so: e.activation(
                st[:, so + 4:so + 5], st[:, so + 5:so + 6], AF.Exp, scale=-0.5),
                inc=tA)
            P.wait("vector", tA, nrstd)
            if xnt_free[par]:
                P.wait("vector", tA, xnt_free[par])
            P.addf("vector", lambda e, so=so: e.scalar_tensor_tensor(
                st[:, so + 5:so + 6], st[:, so + 2:so + 3], -1.0,
                st[:, so + 4:so + 5], ALU.mult, ALU.mult))
            xnt = self.xnt[:, par * C:(par + 1) * C]
            nxn = P.add("vector", lambda e, xt=xt, xnt=xnt, so=so: e.tensor_scalar(
                xnt, xt, st[:, so + 4:so + 5], st[:, so + 5:so + 6],
                ALU.mult, ALU.add), inc=tV)
            dve_read[par] = nxn
            P.wait("tensor", tV, nxn)
            if evac_done[par]:
                P.wait("tensor", tA, evac_done[par])
            pk = packs[par]
            for ck in range(NC):
                P.add("tensor", lambda e, ck=ck, pk=pk, xnt=xnt: e.transpose(
                    self.psb(pk, 128, base=ck * 128), xnt[:, ck * 128:(ck + 1) * 128],
                    self.ident_bf), inc=(tT if ck == NC - 1 else None))
            P.wait("scalar", tT, tT.n)
            dst = self.xnb.rearrange("p (ck t) -> p ck t", ck=NC)[:, :, n * 128:(n + 1) * 128]
            srcv = self.psb(pk, 1024).rearrange("p (ck t) -> p ck t", ck=NC)
            ne = P.add("scalar", lambda e, dst=dst, srcv=srcv: e.activation(
                dst, srcv, AF.Copy), inc=tA)
            evac_done[par] = ne
            xnt_free[par] = ne
        self.ln_tA = tA.n

    def _ffn(self):
        P = self.P
        qG, tV, tA, tT = P.sem("qG"), P.sem("tV"), P.sem("tA"), P.sem("tT")
        xr = self.av(0, 128, NC * T, BF16)
        xk = self.av(32 * KB, 128, NC * T, BF16)
        kact = self.av(64 * KB, 128, NC * T, BF16)
        r_bf = self.av(32 * KB, 128, NC * T, BF16)
        ffn_C = self.av(0, 128, NC * T, BF16)
        wbuf = self.av(96 * KB, 128, NC * 1024, BF16)
        self.ffn_C = ffn_C

        P.addf("vector", lambda e: e.memset(self.px[:, 0:1], 0.0))
        P.wait("vector", P.sem("tA"), self.ln_tA)
        d = self.scr[:, 0:2048]
        xr_v = xr.rearrange("p (ck t) -> p ck t", ck=NC)
        xk_v = xk.rearrange("p (ck t) -> p ck t", ck=NC)
        for ck in range(NC):
            src = self.xnb[:, ck * T:(ck + 1) * T]
            P.addf("vector", lambda e, src=src: e.tensor_copy(self.px[:, 1:T + 1], src))
            P.addf("vector", lambda e: e.tensor_sub(d, self.px[:, 1:T + 1],
                                                    self.px[:, 0:T]))
            P.add("vector", lambda e, ck=ck: e.scalar_tensor_tensor(
                xr_v[:, ck], d, self.mu_pp[:, ck:ck + 1], self.px[:, 0:T],
                ALU.mult, ALU.add))
            self.mix_done = P.add("vector", lambda e, ck=ck: e.scalar_tensor_tensor(
                xk_v[:, ck], d, self.mu_pp[:, 8 + ck:9 + ck], self.px[:, 0:T],
                ALU.mult, ALU.add), inc=tV)

        def load_w(name, wait_t):
            P.wait("gpsimd", tT, wait_t)
            return P.add("gpsimd", lambda e, a=self.g[name]: e.dma_start(
                out=wbuf, in_=a), inc=qG, k=16)

        banks8 = [self.pbA, self.pbB, self.pbC, self.pbD] + [self.ps_s] * 4
        bank_base = [0, 0, 0, 0, 0, 512, 1024, 1536]

        def bank(i):
            return banks8[i], bank_base[i]

        nwk = load_w("w_k", 0)
        P.wait("tensor", qG, nwk)
        P.wait("tensor", tV, self.mix_done)
        kact_v = kact.rearrange("p (dk t) -> p dk t", dk=NC)
        tmp = [self.scr[:, 0:512], self.scr[:, 512:1024]]
        tmp_tA = [0, 0]
        bank_free = [0] * 8
        for dk in range(NC):
            for ns in range(4):
                bi = (dk % 2) * 4 + ns
                bt, bb = bank(bi)
                P.wait("tensor", tV, bank_free[bi])
                for ck in range(NC):
                    P.add("tensor", lambda e, dk=dk, ns=ns, ck=ck, bt=bt, bb=bb:
                          e.matmul(self.psf(bt, 512, base=bb),
                                   wbuf[:, ck * 1024 + dk * 128:ck * 1024 + (dk + 1) * 128],
                                   xk[:, ck * T + ns * 512:ck * T + (ns + 1) * 512],
                                   start=(ck == 0), stop=(ck == NC - 1)),
                          inc=(tT if ck == NC - 1 else None))
                nmm = tT.n
                wpar = (dk * 4 + ns) % 2
                P.wait("vector", tT, nmm)
                P.wait("vector", tA, tmp_tA[wpar])
                nrelu = P.add("vector", lambda e, bt=bt, bb=bb, wpar=wpar:
                              e.tensor_scalar(tmp[wpar], self.psf(bt, 512, base=bb),
                                              0.0, None, ALU.max), inc=tV)
                bank_free[bi] = nrelu
                P.wait("scalar", tV, nrelu)
                tmp_tA[wpar] = P.add("scalar", lambda e, dk=dk, ns=ns, wpar=wpar:
                                     e.activation(kact_v[:, dk, ns * 512:(ns + 1) * 512],
                                                  tmp[wpar], AF.Square), inc=tA)
        k_mm_done = tT.n
        kact_tA = tA.n

        nwr = load_w("w_r", k_mm_done)
        P.wait("tensor", qG, nwr)
        rbf_v = r_bf.rearrange("p (dk t) -> p dk t", dk=NC)
        r_bank_tA = [0] * 4
        for dk in range(NC):
            for ns in range(4):
                bi = ns
                bt, bb = bank(bi)
                P.wait("tensor", tV, bank_free[bi])
                P.wait("tensor", tA, r_bank_tA[bi])
                for ck in range(NC):
                    P.add("tensor", lambda e, dk=dk, ns=ns, ck=ck, bt=bt, bb=bb:
                          e.matmul(self.psf(bt, 512, base=bb),
                                   wbuf[:, ck * 1024 + dk * 128:ck * 1024 + (dk + 1) * 128],
                                   xr[:, ck * T + ns * 512:ck * T + (ns + 1) * 512],
                                   start=(ck == 0), stop=(ck == NC - 1)),
                          inc=(tT if ck == NC - 1 else None))
                nmm = tT.n
                P.wait("scalar", tT, nmm)
                r_bank_tA[bi] = P.add("scalar", lambda e, dk=dk, ns=ns, bt=bt, bb=bb:
                                      e.activation(rbf_v[:, dk, ns * 512:(ns + 1) * 512],
                                                   self.psf(bt, 512, base=bb),
                                                   AF.Sigmoid), inc=tA)
        r_mm_done = tT.n
        sig_all = tA.n

        nwv = load_w("w_v", r_mm_done)
        P.wait("tensor", qG, nwv)
        P.wait("tensor", tA, kact_tA)
        ffnC_v = ffn_C.rearrange("p (dk t) -> p dk t", dk=NC)
        v_bank_tV = [bank_free[4 + i] for i in range(4)]
        v_bank_tV = [0, 0, 0, 0]
        for i in range(4):
            v_bank_tV[i] = bank_free[4 + i]
        for dk in range(NC):
            for ns in range(4):
                bi = 4 + ns
                bt, bb = bank(bi)
                P.wait("tensor", tV, v_bank_tV[ns])
                for ck in range(NC):
                    P.add("tensor", lambda e, dk=dk, ns=ns, ck=ck, bt=bt, bb=bb:
                          e.matmul(self.psf(bt, 512, base=bb),
                                   wbuf[:, ck * 1024 + dk * 128:ck * 1024 + (dk + 1) * 128],
                                   kact[:, ck * T + ns * 512:ck * T + (ns + 1) * 512],
                                   start=(ck == 0), stop=(ck == NC - 1)),
                          inc=(tT if ck == NC - 1 else None))
                nmm = tT.n
                P.wait("vector", tT, nmm)
                P.wait("vector", tA, sig_all if (dk == 0 and ns == 0) else 0)
                v_bank_tV[ns] = P.add("vector", lambda e, dk=dk, ns=ns, bt=bt, bb=bb:
                                      e.tensor_mul(ffnC_v[:, dk, ns * 512:(ns + 1) * 512],
                                                   self.psf(bt, 512, base=bb),
                                                   rbf_v[:, dk, ns * 512:(ns + 1) * 512]),
                                      inc=tV)
        self.ffn_tV = tV.n
        self.ffn_v_mm_done = tT.n
        self.dbg_store("ffn", ffn_C, (128, NC * T), BF16, [(tV, tV.n)])

    def _attn(self):
        P = self.P
        qG, tV, tA, tT = P.sem("qG"), P.sem("tV"), P.sem("tA"), P.sem("tT")
        wqkv = self.av(32 * KB, 128, NC * 3 * C, BF16)
        yD = self.av(80 * KB, 128, NC * T, BF16)
        QT = self.av(112 * KB, 64, T, BF16)
        KT = self.av(116 * KB, 64, T, BF16)
        VTP = self.av(120 * KB, 128, T, BF16)
        VTm = self.av(124 * KB, 128, NT * HD, BF16)
        pt_sb = [self.av(126 * KB, 128, 128, BF16),
                 self.av(126 * KB + 256, 128, 128, BF16)]
        ytmp = self.av(126 * KB + 512, 128, HD, BF16)
        tm_h = self.px[:, 0:1536].bitcast(BF16)
        self.yD = yD

        nwq = None
        P.wait("gpsimd", tT, self.ffn_v_mm_done)
        nwq = P.add("gpsimd", lambda e: e.dma_start(out=wqkv, in_=self.g["w_qkv"]),
                    inc=qG, k=16)
        P.wait("tensor", tV, tV.n)
        P.wait("tensor", tA, tA.n)
        P.wait("tensor", qG, nwq)
        P.wait("tensor", qG, self.n_const_g)
        st8 = self.stat

        qkv_banks = [self.pbA, self.pbB]
        qkv_free_tA = [0, 0]
        pack_banks = [self.pbC, self.pbD]
        pack_free = [0, 0]
        pack_free_eng = [None, None]
        pv_free_tV = 0
        ptb = [self.pbB, self.pbC]
        pt_free = [0, 0]
        ytr_free_tA = 0
        ytmp_free_tT = 0

        for h in range(NHEAD):
            P.wait("tensor", tV, pv_free_tV)
            P.wait("tensor", tV, pt_free[0])
            P.wait("tensor", tV, pt_free[1])
            P.wait("tensor", tA, ytr_free_tA)
            for slab in range(6):
                par = slab % 2
                bt = qkv_banks[par]
                P.wait("tensor", tA, qkv_free_tA[par])
                for ck in range(NC):
                    P.add("tensor", lambda e, h=h, slab=slab, ck=ck, bt=bt: e.matmul(
                        self.psf(bt, 512),
                        self.xnb[:, ck * T + h * 128:ck * T + (h + 1) * 128],
                        wqkv[:, ck * 3072 + slab * 512:ck * 3072 + (slab + 1) * 512],
                        start=(ck == 0), stop=False))
                nmm = P.add("tensor", lambda e, slab=slab, bt=bt: e.matmul(
                    self.psf(bt, 512), self.ones_bf[0:1, 0:128],
                    self.bqkv_row[0:1, slab * 512:(slab + 1) * 512],
                    start=False, stop=True), inc=tT)
                P.wait("scalar", tT, nmm)
                qkv_free_tA[par] = P.add("scalar", lambda e, slab=slab, bt=bt:
                                         e.activation(tm_h[:, slab * 512:(slab + 1) * 512],
                                                      self.psf(bt, 512), AF.Copy),
                                         inc=tA)
            qkv_done_tA = tA.n

            def emit_pack(src_col0, dst_fm, s_lo, pk_i, on_act):
                pk = pack_banks[pk_i]
                P.wait("tensor", tA, qkv_done_tA)
                if pack_free[pk_i]:
                    P.wait("tensor", P.sem(pack_free_eng[pk_i]), pack_free[pk_i])
                for i in range(8):
                    s = s_lo + i
                    P.add("tensor", lambda e, s=s, i=i, pk=pk, src_col0=src_col0:
                          e.transpose(self.psb(pk, 128, base=i * 128, parts=64),
                                      tm_h[:, src_col0 + s * 64:src_col0 + (s + 1) * 64],
                                      self.ident_bf),
                          inc=(tT if i == 7 else None))
                ntr = tT.n
                eng = "scalar" if on_act else "vector"
                P.wait(eng, tT, ntr)
                dstv = dst_fm.rearrange("d (j s) -> d s j", s=16)[:, s_lo:s_lo + 8, :]
                srcv = self.psb(pk, 1024, parts=64).rearrange("d (s j) -> d s j", s=8)
                if on_act:
                    nn = P.add("scalar", lambda e, dstv=dstv, srcv=srcv:
                               e.activation(dstv, srcv, AF.Copy), inc=tA)
                else:
                    nn = P.add("vector", lambda e, dstv=dstv, srcv=srcv:
                               e.tensor_copy(dstv, srcv), inc=tV)
                pack_free[pk_i] = nn
                pack_free_eng[pk_i] = "tA" if on_act else "tV"
                return nn

            emit_pack(0, QT, 0, 0, False)
            emit_pack(0, QT, 8, 1, False)
            emit_pack(C, KT, 0, 0, False)
            nk = emit_pack(C, KT, 8, 1, False)
            emit_pack(2 * C, VTP[0:64, :], 0, 0, True)
            nvt = emit_pack(2 * C, VTP[0:64, :], 8, 1, True)
            pk = pack_banks[0]
            P.wait("tensor", P.sem(pack_free_eng[0]), pack_free[0])
            P.wait("tensor", tA, nvt)
            for gph in range(16):
                P.add("tensor", lambda e, gph=gph, pk=pk: e.transpose(
                    self.psb(pk, 64, base=gph * 64),
                    VTP[0:64, gph * 128:(gph + 1) * 128],
                    self.ident_bf[0:64, 0:64]),
                    inc=(tT if gph == 15 else None))
            nvtm_tr = tT.n
            P.wait("scalar", tT, nvtm_tr)
            nvtm = P.add("scalar", lambda e, pk=pk: e.activation(
                VTm, self.psb(pk, 1024), AF.Copy), inc=tA)
            pack_free[0] = nvtm
            pack_free_eng[0] = "tA"

            P.wait("tensor", tV, nk)
            P.wait("tensor", tA, nvtm)
            exp_tA_prev = 0
            for qt in range(NT):
                L = (qt + 1) * 128
                if exp_tA_prev:
                    P.wait("tensor", tA, exp_tA_prev)
                nslab = (L + 511) // 512
                for sidx in range(nslab):
                    w = min(512, L - sidx * 512)
                    P.add("tensor", lambda e, qt=qt, sidx=sidx, w=w: e.matmul(
                        self.psf(self.ps_s, w, base=sidx * 512),
                        QT[:, qt * 128:(qt + 1) * 128],
                        KT[:, sidx * 512:sidx * 512 + w],
                        start=True, stop=(qt * 128 < sidx * 512 or qt * 128 >= sidx * 512 + w)))
                nsc = P.add("tensor", lambda e, qt=qt: e.matmul(
                    self.psf(self.ps_s, 128, base=qt * 128),
                    self.ident_bf, self.mask_bf, start=False, stop=True), inc=tT)
                P.wait("vector", tT, nsc)
                P.addf("vector", lambda e, L=L: e.reduce_max(
                    st8[:, 0:1], self.psf(self.ps_s, L), axis=AX.X, negate=True))
                nngm = P.add("vector", lambda e: e.tensor_scalar(
                    st8[:, 3:4], st8[:, 0:1], 0.125, None, ALU.mult), inc=tV)
                P.wait("scalar", tV, nngm)
                nexp = P.add("scalar", lambda e, L=L: e.activation(
                    self._Pv(L), self.psf(self.ps_s, L), AF.Exp, scale=0.125,
                    bias=st8[:, 3:4], accum_out=st8[:, 1:2]), inc=tA)
                exp_tA_prev = nexp
                P.wait("vector", tA, nexp)
                nrinv = P.addf("vector", lambda e: e.reciprocal(
                    st8[:, 2:3], st8[:, 1:2]))
                P.wait("tensor", tA, nexp)
                nkt = qt + 1
                for kt in range(nkt):
                    pp = kt % 2
                    P.wait("tensor", tV, pt_free[pp])
                    P.add("tensor", lambda e, kt=kt, pp=pp: e.transpose(
                        self.psb(ptb[pp], 128), self._Pv(T)[:, kt * 128:(kt + 1) * 128],
                        self.ident_bf), inc=tT)
                    ntr = tT.n
                    P.wait("vector", tT, ntr)
                    pt_free[pp] = P.add("vector", lambda e, pp=pp: e.tensor_copy(
                        pt_sb[pp], self.psb(ptb[pp], 128)), inc=tV)
                    P.wait("tensor", tV, pt_free[pp])
                    if kt == 0:
                        P.wait("tensor", tV, pv_free_tV)
                    P.add("tensor", lambda e, kt=kt, pp=pp, qt=qt: e.matmul(
                        self.psf(self.pbA, HD), pt_sb[pp], VTm[:, kt * HD:(kt + 1) * HD],
                        start=(kt == 0), stop=(kt == qt)), inc=(tT if kt == qt else None))
                npv = tT.n
                P.wait("vector", tT, npv)
                P.wait("vector", tT, ytmp_free_tT)
                npvs = P.add("vector", lambda e: e.tensor_scalar(
                    ytmp, self.psf(self.pbA, HD), st8[:, 2:3], None, ALU.mult),
                    inc=tV)
                pv_free_tV = npvs
                P.wait("tensor", tV, npvs)
                P.wait("tensor", tA, ytr_free_tA)
                ytmp_free_tT = P.add("tensor", lambda e: e.transpose(
                    self.psb(self.pbD, 128, parts=64), ytmp, self.ident_bf), inc=tT)
                P.wait("scalar", tT, ytmp_free_tT)
                dst = yD[(h % 2) * 64:(h % 2) * 64 + 64,
                         (h // 2) * T + qt * 128:(h // 2) * T + (qt + 1) * 128]
                ytr_free_tA = P.add("scalar", lambda e, dst=dst: e.activation(
                    dst, self.psb(self.pbD, 128, parts=64), AF.Copy), inc=tA)
        self.attn_tA = tA.n
        self.attn_mm_done = tT.n
        self.attn_pv_tV = pv_free_tV
        self.attn_pt_tV = max(pt_free)
        self.attn_ytr_tA = ytr_free_tA
        self.dbg_store("yd", yD, (128, NC * T), BF16, [(tA, tA.n)])

    def _Pv(self, L):
        return self.av(120 * KB, 128, L, BF16)

    def _proj_resid(self):
        P = self.P
        qG, qS, tV, tA, tT = (P.sem(s) for s in ("qG", "qS", "tV", "tA", "tT"))
        qXo = P.sem("qXo")
        wproj = self.av(32 * KB, 128, NC * C, BF16)
        x1s = [self.av(48 * KB, 128, C, F32), self.av(52 * KB, 128, C, F32)]
        ffn_C = self.ffn_C
        yD = self.yD

        P.wait("gpsimd", tT, self.attn_mm_done)
        nwp = P.add("gpsimd", lambda e: e.dma_start(out=wproj, in_=self.g["w_proj"]),
                    inc=qG, k=16)
        P.wait("tensor", tV, tV.n)
        P.wait("tensor", tA, tA.n)
        P.wait("tensor", qG, nwp)
        banks = [self.pbA, self.pbB]
        bank_free_tV = [0, 0]
        comb_tV = 0
        for cko in range(NC):
            for ns in range(4):
                par = (cko * 4 + ns) % 2
                bt = banks[par]
                P.wait("tensor", tV, bank_free_tV[par])
                for ck in range(NC):
                    P.add("tensor", lambda e, cko=cko, ns=ns, ck=ck, bt=bt: e.matmul(
                        self.psf(bt, 512),
                        wproj[:, ck * C + cko * 128:ck * C + (cko + 1) * 128],
                        yD[:, ck * T + ns * 512:ck * T + (ns + 1) * 512],
                        start=(ck == 0), stop=(ck == NC - 1)),
                        inc=(tT if ck == NC - 1 else None))
                nmm = tT.n
                P.wait("vector", tT, nmm)
                cv = ffn_C.rearrange("p (ck t) -> p ck t", ck=NC)[:, cko,
                                                                  ns * 512:(ns + 1) * 512]
                bank_free_tV[par] = P.add("vector", lambda e, cko=cko, cv=cv, bt=bt:
                                          e.scalar_tensor_tensor(
                                              cv, self.psf(bt, 512),
                                              self.bproj_pp[:, cko:cko + 1], cv,
                                              ALU.add, ALU.add), inc=tV)
        comb_tV = tV.n

        self.x1_tile_ready = []
        packs = [self.pbC, self.pbD]
        pack_free_tV = [0, 0]
        x1s_free = [0, 0]
        xt_slots = [self.scr[:, 0:1024], self.scr[:, 1024:2048]]
        xt_read_tV = [0, 0]
        P.wait("tensor", tV, comb_tV)
        P.wait("sync", tV, comb_tV)
        P.wait("sync", tA, tA.n)
        for n in range(NT):
            par = n % 2
            P.wait("sync", tV, xt_read_tV[par])
            ldn = P.add("sync", lambda e, n=n, par=par: e.dma_start(
                out=xt_slots[par], in_=self.x_in[n * 128:(n + 1) * 128, :]),
                inc=qS, k=16)
            pk = packs[par]
            P.wait("tensor", tV, pack_free_tV[par])
            for ck in range(NC):
                P.add("tensor", lambda e, n=n, ck=ck, pk=pk: e.transpose(
                    self.psb(pk, 128, base=ck * 128),
                    ffn_C[:, ck * T + n * 128:ck * T + (n + 1) * 128],
                    self.ident_bf), inc=(tT if ck == NC - 1 else None))
            ntr = tT.n
            P.wait("vector", tT, ntr)
            P.wait("vector", qS, ldn)
            P.wait("vector", qXo, x1s_free[par])
            nadd = P.add("vector", lambda e, par=par, pk=pk: e.tensor_add(
                x1s[par], xt_slots[par], self.psb(pk, 1024)), inc=tV)
            pack_free_tV[par] = nadd
            xt_read_tV[par] = nadd
            P.wait("sync", tV, nadd)
            nst = P.add("sync", lambda e, n=n, par=par: e.dma_start(
                out=self.xo[n * 128:(n + 1) * 128, :], in_=x1s[par]), inc=qXo, k=16)
            x1s_free[par] = nst
            self.x1_tile_ready.append((qXo, nst))

    def _qkv1(self):
        P = self.P
        qG, qS, tV, tA, tT = (P.sem(s) for s in ("qG", "qS", "tV", "tA", "tT"))
        qQKV = P.sem("qQKV")
        wq1 = self.av(112 * KB, 128, NC * HD, BF16)
        wk1 = self.av(113 * KB, 128, NC * HD, BF16)
        wv1 = self.av(114 * KB, 128, NC * HD, BF16)
        v1t = self.av(115 * KB, 128, NT * HD, BF16)
        q1 = self.av(117 * KB, 64, T, BF16)
        k1 = self.av(121 * KB, 64, T, BF16)
        v1f = self.av(125 * KB, 64, T, BF16)

        P.wait("gpsimd", tT, self.attn_mm_done)
        nl = None
        for w, name in ((wq1, "w_q1"), (wk1, "w_k1"), (wv1, "w_v1")):
            nl = P.add("gpsimd", lambda e, w=w, a=self.g[name]: e.dma_start(
                out=w, in_=a), inc=qG, k=16)
        P.wait("tensor", tV, P.sem("tV").n)
        P.wait("tensor", tA, self.ln_tA)
        P.wait("tensor", qG, nl)
        banks = [self.pbA, self.pbB]
        bank_free_tV = [0, 0]
        outs = [(wq1, q1, 0), (wk1, k1, 1), (wv1, v1f, 2)]
        for wi, (w, dstf, bcol) in enumerate(outs):
            for ns in range(4):
                par = ns % 2
                bt = banks[par]
                P.wait("tensor", tV, bank_free_tV[par])
                for ck in range(NC):
                    P.add("tensor", lambda e, w=w, ck=ck, ns=ns, bt=bt: e.matmul(
                        self.psf(bt, 512, parts=64),
                        w[:, ck * HD:(ck + 1) * HD],
                        self.xnb[:, ck * T + ns * 512:ck * T + (ns + 1) * 512],
                        start=(ck == 0), stop=(ck == NC - 1)),
                        inc=(tT if ck == NC - 1 else None))
                nmm = tT.n
                P.wait("vector", tT, nmm)
                bank_free_tV[par] = P.add("vector", lambda e, dstf=dstf, ns=ns,
                                          bcol=bcol, bt=bt: e.tensor_scalar(
                    dstf[:, ns * 512:(ns + 1) * 512], self.psf(bt, 512, parts=64),
                    self.bqkv1_pp[:, bcol:bcol + 1], None, ALU.add), inc=tV)
        qkv_tV = tV.n
        P.wait("tensor", tV, qkv_tV)
        for gph in range(16):
            P.add("tensor", lambda e, gph=gph: e.transpose(
                self.psb(self.pbC, 64, base=gph * 64),
                v1f[:, gph * 128:(gph + 1) * 128], self.ident_bf[0:64, 0:64]),
                inc=(tT if gph == 15 else None))
        nvt = tT.n
        P.wait("scalar", tT, nvt)
        nvta = P.add("scalar", lambda e: e.activation(v1t, self.psb(self.pbC, 1024),
                                                      AF.Copy), inc=tA)
        P.wait("sync", tV, qkv_tV)
        P.add("sync", lambda e: e.dma_start(out=self.qo, in_=q1), inc=qQKV, k=16)
        P.add("sync", lambda e: e.dma_start(out=self.ko, in_=k1), inc=qQKV, k=16)
        P.wait("sync", tA, nvta)
        P.add("sync", lambda e: e.dma_start(out=self.vo, in_=v1t), inc=qQKV, k=16)
        self.qkv1_pbc_tA = nvta

    def _gate(self):
        P = self.P
        tV, tA, tT, qG = P.sem("tV"), P.sem("tA"), P.sem("tT"), P.sem("qG")
        gs = self.gsc
        P.add("vector", lambda e: e.memset(
            self.rw_tm.rearrange("p (n e) -> p n e", e=NEXP)[:, :, 8], 1.0))
        P.wait("vector", tV, self.gbb_done)
        P.wait("tensor", qG, self.n_const_g)
        exp_tA = 0
        rwfm_tV = 0
        for n in range(NT):
            P.wait("tensor", tA, exp_tA)
            P.wait("tensor", tA, self.qkv1_pbc_tA if n == 0 else 0)
            for ck in range(NC):
                P.add("tensor", lambda e, n=n, ck=ck: e.matmul(
                    self.psf(self.pbD, 8),
                    self.xnb[:, ck * T + n * 128:ck * T + (n + 1) * 128],
                    self.wgate[:, ck * 8:(ck + 1) * 8],
                    start=(ck == 0), stop=(ck == NC - 1)),
                    inc=(tT if ck == NC - 1 else None))
            nmm = tT.n
            P.wait("vector", tT, nmm)
            nng = P.add("vector", lambda e: e.reduce_max(
                gs[:, 24:25], self.psf(self.pbD, 8), axis=AX.X, negate=True), inc=tV)
            P.wait("scalar", tV, nng)
            exp_tA = P.add("scalar", lambda e: e.activation(
                gs[:, 0:8], self.psf(self.pbD, 8), AF.Exp, bias=gs[:, 24:25],
                accum_out=gs[:, 27:28]), inc=tA)
            P.wait("vector", tA, exp_tA)
            P.addf("vector", lambda e: e.reciprocal(gs[:, 28:29], gs[:, 27:28]))
            P.addf("vector", lambda e: e.tensor_scalar(
                gs[:, 0:8], gs[:, 0:8], gs[:, 28:29], None, ALU.mult))
            P.addf("vector", lambda e: e.tensor_add(gs[:, 8:16], gs[:, 0:8], self.gbb))
            P.addf("vector", lambda e: e.reduce_max(gs[:, 24:25], gs[:, 8:16], axis=AX.X))
            P.addf("vector", lambda e: e.tensor_scalar(
                gs[:, 16:24], gs[:, 8:16], gs[:, 24:25], None, ALU.is_ge))
            P.addf("vector", lambda e: e.scalar_tensor_tensor(
                gs[:, 16:24], gs[:, 16:24], -1e30, gs[:, 8:16], ALU.mult, ALU.add))
            P.addf("vector", lambda e: e.reduce_max(gs[:, 25:26], gs[:, 16:24], axis=AX.X))
            P.addf("vector", lambda e: e.tensor_scalar(
                gs[:, 16:24], gs[:, 8:16], gs[:, 25:26], None, ALU.is_ge))
            rw = self.rw_tm[:, n * NEXP:n * NEXP + 8]
            nrw = P.add("vector", lambda e, rw=rw: e.tensor_mul(
                rw, gs[:, 0:8], gs[:, 16:24]), inc=tV)
            P.wait("tensor", tV, nrw)
            P.wait("tensor", tV, rwfm_tV)
            ntr = P.add("tensor", lambda e, n=n: e.transpose(
                self.psf(self.pbC, 128, parts=NEXP),
                self.rw_tm[:, n * NEXP:(n + 1) * NEXP], self.ident_f32), inc=tT)
            P.wait("vector", tT, ntr)
            rwfm_tV = P.add("vector", lambda e, n=n: e.tensor_copy(
                self.rw_fm[:, n * 128:(n + 1) * 128],
                self.psf(self.pbC, 128, parts=NEXP)), inc=tV)
        self.gate_tV = tV.n
        self.gate_mm_tT = tT.n
        self.dbg_store("rw", self.rw_tm, (128, NT * NEXP), F32, [(tV, tV.n)])

    def _moe(self):
        P = self.P
        qG, tV, tA, tT = P.sem("qG"), P.sem("tV"), P.sem("tA"), P.sem("tT")
        qM, qXo = P.sem("qM"), P.sem("qXo")
        h_buf = self.av(0, 128, NHK * T, BF16)
        blk = [[self.av(44 * KB + (i * 2 + p) * 2 * KB, 128, 1024, BF16)
                for p in range(2)] for i in range(2)]
        w2 = self.av(52 * KB, 128, NHK * C, BF16)
        stg = [self.av(74 * KB, 128, C, F32), self.av(78 * KB, 128, C, F32)]

        tile_chain = [self.x1_tile_ready[n] for n in range(NT)]
        P.wait("gpsimd", qXo, P.sem("qXo").n)

        P.wait("tensor", tV, self.gate_tV)
        P.wait("tensor", qG, self.n_const_g)
        stg_free = [0, 0]
        psq_free = [0, 0, 0, 0]
        for n in range(NT):
            par = n % 2
            for cs in range(2):
                qi = par * 2 + cs
                P.wait("tensor", tV, psq_free[qi])
                P.add("tensor", lambda e, n=n, cs=cs, qi=qi: e.matmul(
                    self.psf(self.ps_s, 512, base=qi * 512),
                    self.rw_fm[:, n * 128:(n + 1) * 128],
                    self.b2cat[:, cs * 512:(cs + 1) * 512],
                    start=True, stop=True), inc=(tT if cs == 1 else None))
            nmm = tT.n
            P.wait("vector", tT, nmm)
            P.wait("vector", qM, stg_free[par])
            ncp = None
            for cs in range(2):
                qi = par * 2 + cs
                psq_free[qi] = P.add("vector", lambda e, par=par, cs=cs, qi=qi:
                                     e.tensor_copy(stg[par][:, cs * 512:(cs + 1) * 512],
                                                   self.psf(self.ps_s, 512, base=qi * 512)),
                                     inc=tV)
            ncp = psq_free[par * 2 + 1]
            P.wait("gpsimd", tV, ncp)
            cs_, cn_ = tile_chain[n]
            P.wait("gpsimd", cs_, cn_)
            nadd = P.add("gpsimd", lambda e, n=n, par=par: e.dma_start(
                out=self.xo[n * 128:(n + 1) * 128, :], in_=stg[par],
                accum_op=ALU.add), inc=qM, k=16)
            stg_free[par] = nadd
            tile_chain[n] = (qM, nadd)

        hv = h_buf.rearrange("p (hk t) -> p hk t", hk=NHK)
        w2v = w2.rearrange("p (hk c) -> p hk c", hk=NHK)
        esA = [self.scr[:, 0:512], self.scr[:, 512:1024]]
        esB = [self.scr[:, 1024:1536], self.scr[:, 1536:2048]]
        blk_free = [[0, 0], [0, 0]]
        w2_free = 0
        h_free = 0
        bankA = [(self.pbA, self.pbB), (self.pbC, self.pbD)]
        h1_free = [0, 0]
        h3_free = [0, 0]
        es_free = [0, 0]
        psh_free = [0, 0]
        P.wait("tensor", tV, tV.n)
        P.wait("tensor", tA, tA.n)
        for e_i in range(NEXP):
            P.wait("gpsimd", tT, w2_free)
            nw2 = None
            for hk in range(NHK):
                nw2 = P.add("gpsimd", lambda e, e_i=e_i, hk=hk: e.dma_start(
                    out=w2v[:, hk], in_=self.g["w2b"][e_i * NHK + hk]),
                    inc=qG, k=16)
            w2_loaded = nw2
            P.wait("scalar", tT, h_free)
            for hk in range(NHK):
                par = hk % 2
                P.wait("gpsimd", tT, blk_free[0][par])
                n1 = P.add("gpsimd", lambda e, e_i=e_i, hk=hk, par=par: e.dma_start(
                    out=blk[0][par], in_=self.g["w1b"][e_i * NHK + hk]), inc=qG, k=16)
                P.wait("gpsimd", tT, blk_free[1][par])
                n3 = P.add("gpsimd", lambda e, e_i=e_i, hk=hk, par=par: e.dma_start(
                    out=blk[1][par], in_=self.g["w3b"][e_i * NHK + hk]), inc=qG, k=16)
                bt1, bt3 = bankA[par]
                P.wait("tensor", qG, n3)
                for ns in range(4):
                    P.wait("tensor", tV, h1_free[par])
                    for ck in range(NC):
                        P.add("tensor", lambda e, hk=hk, ns=ns, ck=ck, par=par, bt1=bt1:
                              e.matmul(self.psf(bt1, 512),
                                       blk[0][par][:, ck * 128:(ck + 1) * 128],
                                       self.xnb[:, ck * T + ns * 512:ck * T + (ns + 1) * 512],
                                       start=(ck == 0), stop=(ck == NC - 1)),
                              inc=(tT if ck == NC - 1 else None))
                    n1mm = tT.n
                    P.wait("tensor", tA, h3_free[par])
                    for ck in range(NC):
                        P.add("tensor", lambda e, hk=hk, ns=ns, ck=ck, par=par, bt3=bt3:
                              e.matmul(self.psf(bt3, 512),
                                       blk[1][par][:, ck * 128:(ck + 1) * 128],
                                       self.xnb[:, ck * T + ns * 512:ck * T + (ns + 1) * 512],
                                       start=(ck == 0), stop=(ck == NC - 1)),
                              inc=(tT if ck == NC - 1 else None))
                    n3mm = tT.n
                    bcol = e_i * NHK + hk
                    P.wait("vector", tT, n1mm)
                    P.wait("vector", tA, es_free[par])
                    nt1 = P.add("vector", lambda e, par=par, bcol=bcol, bt1=bt1:
                                e.tensor_scalar(esA[par], self.psf(bt1, 512),
                                                self.b1_pp[:, bcol:bcol + 1], None,
                                                ALU.add), inc=tV)
                    P.wait("vector", tV, nt1)
                    h1_free[par] = nt1
                    P.wait("scalar", tT, n3mm)
                    nt2 = P.add("scalar", lambda e, par=par, bcol=bcol, bt3=bt3:
                                e.activation(esB[par], self.psf(bt3, 512), AF.Identity,
                                             bias=self.b3_pp[:, bcol:bcol + 1]),
                                inc=tA)
                    h3_free[par] = nt2
                    P.wait("vector", tA, nt2)
                    nt3 = P.add("vector", lambda e, par=par: e.tensor_mul(
                        esA[par], esA[par], esB[par]), inc=tV)
                    P.wait("scalar", tV, nt3)
                    nsl = P.add("scalar", lambda e, hk=hk, ns=ns, par=par:
                                e.activation(hv[:, hk, ns * 512:(ns + 1) * 512],
                                             esA[par], AF.Silu), inc=tA)
                    es_free[par] = nsl
                blk_free[0][par] = n1mm
                blk_free[1][par] = n3mm
            silu_all = P.sem("tA").n
            P.wait("tensor", tA, silu_all)
            P.wait("tensor", qG, w2_loaded)
            for tq in range(NT):
                par = tq % 2
                P.wait("tensor", tV, psh_free[par])
                for hk in range(NHK):
                    for cs in range(2):
                        P.add("tensor", lambda e, tq=tq, hk=hk, cs=cs, par=par:
                              e.matmul(self.psf(self.ps_s, 512, base=par * 1024 + cs * 512),
                                       hv[:, hk, tq * 128:(tq + 1) * 128],
                                       w2v[:, hk, cs * 512:(cs + 1) * 512],
                                       start=(hk == 0), stop=(hk == NHK - 1)),
                              inc=(tT if hk == NHK - 1 and cs == 1 else None))
                nmm = tT.n
                P.wait("vector", tT, nmm)
                P.wait("vector", qM, stg_free[par])
                rwcol = self.rw_tm[:, tq * NEXP + e_i:tq * NEXP + e_i + 1]
                nev = None
                for cs in range(2):
                    nev = P.add("vector", lambda e, par=par, cs=cs, rwcol=rwcol:
                                e.tensor_scalar(stg[par][:, cs * 512:(cs + 1) * 512],
                                                self.psf(self.ps_s, 512,
                                                         base=par * 1024 + cs * 512),
                                                rwcol, None, ALU.mult), inc=tV)
                psh_free[par] = nev
                P.wait("gpsimd", tV, nev)
                cs_, cn_ = tile_chain[tq]
                P.wait("gpsimd", cs_, cn_)
                nadd = P.add("gpsimd", lambda e, tq=tq, par=par: e.dma_start(
                    out=self.xo[tq * 128:(tq + 1) * 128, :], in_=stg[par],
                    accum_op=ALU.add), inc=qM, k=16)
                stg_free[par] = nadd
                tile_chain[tq] = (qM, nadd)
            w2_free = tT.n
            h_free = tT.n



class BuildB:
    def __init__(self):
        nc = bass.Bass("TRN2", target_bir_lowering=False)
        self.nc = nc
        P = Prog(nc)
        self.P = P
        xin = nc.dram_tensor("xin", [T, C], F32, kind="ExternalInput").ap()
        qin = nc.dram_tensor("q", [HD, T], BF16, kind="ExternalInput").ap()
        kall = nc.dram_tensor("kall", [HD, GTOK], BF16, kind="ExternalInput").ap()
        vall = nc.dram_tensor("vall", [128, (GTOK // 128) * HD], BF16,
                              kind="ExternalInput").ap()
        identd = nc.dram_tensor("ident_bf", [128, 128], BF16,
                                kind="ExternalInput").ap()
        out = nc.dram_tensor("out", [T, C], F32, kind="ExternalOutput").ap()

        sbt = nc.alloc_sbuf_tensor
        ident = sbt("ident", [128, 128], BF16).ap()
        q1 = sbt("q1", [HD, T], BF16).ap()
        ka = sbt("ka", [HD, GTOK], BF16).ap()
        va = sbt("va", [128, (GTOK // 128) * HD], BF16).ap()
        Pt = sbt("Pt", [128, 2 * 512], BF16).ap()
        pt_sb = sbt("pt_sb", [128, 2 * 128], BF16).ap()
        acc = sbt("acc", [128, HD], F32).ap()
        yt = sbt("yt", [128, HD], F32).ap()
        stt = sbt("stt", [128, 8], F32).ap()
        xts = sbt("xts", [128, 2 * C], F32).ap()
        ots = sbt("ots", [128, 2 * C], F32).ap()

        ps_sc = nc.alloc_psum_tensor("ps_sc", [128, 1024], F32)
        pv = nc.alloc_psum_tensor("pv", [128, 512], F32)
        ptp = [nc.alloc_psum_tensor("ptp0", [128, 512], F32),
               nc.alloc_psum_tensor("ptp1", [128, 512], F32)]

        qS, qO, tV, tA, tT = (P.sem(s) for s in ("qS", "qO", "tV", "tA", "tT"))

        for t, a in ((ident, identd), (q1, qin), (ka, kall), (va, vall)):
            P.add("sync", lambda e, t=t, a=a: e.dma_start(out=t, in_=a), inc=qS, k=16)
        n_ld = qS.n
        P.wait("tensor", qS, n_ld)

        NCH = GTOK // 512
        sc_free_tA = [0, 0]
        pt_free_tV = [0, 0]
        p_free_tT = [0, 0]
        pv_read_tV = 0
        yt_read_tV = 0
        xt_read = [0, 0]
        ot_free = [0, 0]
        for qt in range(NT):
            par_t = qt % 2
            P.wait("sync", tV, xt_read[par_t])
            ldx = P.add("sync", lambda e, qt=qt, par_t=par_t: e.dma_start(
                out=xts[:, par_t * C:(par_t + 1) * C],
                in_=xin[qt * 128:(qt + 1) * 128, :]), inc=qS, k=16)
            for ci in range(NCH):
                par = ci % 2
                P.wait("tensor", tA, sc_free_tA[par])
                nsc = P.add("tensor", lambda e, qt=qt, ci=ci, par=par: e.matmul(
                    ps_sc.ap()[:, par * 512:(par + 1) * 512],
                    q1[:, qt * 128:(qt + 1) * 128],
                    ka[:, ci * 512:(ci + 1) * 512], start=True, stop=True), inc=tT)
                P.wait("vector", tT, nsc)
                P.addf("vector", lambda e, par=par: e.reduce_max(
                    stt[:, 3:4], ps_sc.ap()[:, par * 512:(par + 1) * 512],
                    axis=AX.X, negate=True))
                if ci == 0:
                    P.addf("vector", lambda e: e.tensor_copy(
                        stt[:, 0:1], stt[:, 3:4]))
                    nb = P.add("vector", lambda e: e.tensor_scalar(
                        stt[:, 4:5], stt[:, 0:1], 1.0 / 32.0, None, ALU.mult),
                        inc=tV)
                else:
                    P.addf("vector", lambda e: e.tensor_scalar(
                        stt[:, 5:6], stt[:, 3:4], stt[:, 0:1], 0.0,
                        ALU.subtract, ALU.min))
                    nb = P.add("vector", lambda e: e.tensor_scalar(
                        stt[:, 4:5], stt[:, 5:6], stt[:, 0:1], 1.0 / 32.0,
                        ALU.add, ALU.mult), inc=tV)
                    P.addf("vector", lambda e: e.tensor_scalar(
                        stt[:, 0:1], stt[:, 5:6], stt[:, 0:1], None, ALU.add))
                P.wait("scalar", tV, nb)
                if ci > 0:
                    P.add("scalar", lambda e: e.activation(
                        stt[:, 6:7], stt[:, 5:6], AF.Exp, scale=1.0 / 32.0))
                P.wait("scalar", tT, p_free_tT[par])
                nexp = P.add("scalar", lambda e, par=par: e.activation(
                    Pt[:, par * 512:(par + 1) * 512],
                    ps_sc.ap()[:, par * 512:(par + 1) * 512], AF.Exp,
                    scale=1.0 / 32.0, bias=stt[:, 4:5], accum_out=stt[:, 7:8]),
                    inc=tA)
                sc_free_tA[par] = nexp
                P.wait("vector", tA, nexp)
                if ci == 0:
                    P.addf("vector", lambda e: e.tensor_copy(stt[:, 1:2], stt[:, 7:8]))
                else:
                    P.addf("vector", lambda e: e.scalar_tensor_tensor(
                        stt[:, 1:2], stt[:, 1:2], stt[:, 6:7], stt[:, 7:8],
                        ALU.mult, ALU.add))
                for kt in range(4):
                    pp = kt % 2
                    P.wait("tensor", tV, pt_free_tV[pp])
                    ntr = P.add("tensor", lambda e, kt=kt, pp=pp, par=par: e.transpose(
                        ptp[pp].ap().bitcast(BF16)[:, 0:128],
                        Pt[:, par * 512 + kt * 128:par * 512 + (kt + 1) * 128],
                        ident), inc=tT)
                    P.wait("vector", tT, ntr)
                    pt_free_tV[pp] = P.add("vector", lambda e, pp=pp: e.tensor_copy(
                        pt_sb[:, pp * 128:(pp + 1) * 128],
                        ptp[pp].ap().bitcast(BF16)[:, 0:128]), inc=tV)
                    P.wait("tensor", tV, pt_free_tV[pp])
                    if kt == 0:
                        P.wait("tensor", tV, pv_read_tV)
                    gk = ci * 4 + kt
                    P.add("tensor", lambda e, gk=gk, pp=pp: e.matmul(
                        pv.ap()[:, 0:HD], pt_sb[:, pp * 128:(pp + 1) * 128],
                        va[:, gk * HD:(gk + 1) * HD],
                        start=(gk % 4 == 0), stop=(gk % 4 == 3)),
                        inc=(tT if kt == 3 else None))
                p_free_tT[par] = tT.n
                npv = tT.n
                P.wait("vector", tT, npv)
                if ci == 0:
                    pv_read_tV = P.add("vector", lambda e: e.tensor_copy(
                        acc, pv.ap()[:, 0:HD]), inc=tV)
                else:
                    pv_read_tV = P.add("vector", lambda e: e.scalar_tensor_tensor(
                        acc, acc, stt[:, 6:7], pv.ap()[:, 0:HD],
                        ALU.mult, ALU.add), inc=tV)
                P.wait("vector", tV, pv_read_tV)
            P.addf("vector", lambda e: e.reciprocal(stt[:, 2:3], stt[:, 1:2]))
            nyt = P.addf("vector", lambda e: e.tensor_scalar(
                yt, acc, stt[:, 2:3], None, ALU.mult))
            P.wait("vector", qS, ldx)
            P.wait("vector", qO, ot_free[par_t])
            ot = ots[:, par_t * C:(par_t + 1) * C]
            not_ = P.add("vector", lambda e, par_t=par_t, ot=ot: e.tensor_add(
                ot.rearrange("p (r d) -> p r d", d=HD),
                xts[:, par_t * C:(par_t + 1) * C].rearrange("p (r d) -> p r d", d=HD),
                yt.unsqueeze(1).broadcast_to([128, NHEAD, HD])), inc=tV)
            xt_read[par_t] = not_
            yt_read_tV = not_
            P.wait("sync", tV, not_)
            nst = P.add("sync", lambda e, qt=qt, ot=ot: e.dma_start(
                out=out[qt * 128:(qt + 1) * 128, :], in_=ot), inc=qO, k=16)
            ot_free[par_t] = nst
        P.wait("sync", qO, P.sem("qO").n)
        P.finalize()



_BUILD_CACHE = {}


def kernel(**inputs):
    g = make_host_globals(inputs)
    x = np.asarray(inputs["L0_attn_x"], np.float32)

    if "A" not in _BUILD_CACHE:
        _BUILD_CACHE["A"] = BuildA()
        _BUILD_CACHE["B"] = BuildB()
    A, B = _BUILD_CACHE["A"], _BUILD_CACHE["B"]

    in_maps = []
    for b in range(NCORE):
        m = dict(g)
        m["x"] = np.ascontiguousarray(x[b])
        in_maps.append(m)
    resA = run_bass_kernel_spmd(A.nc, in_maps, core_ids=list(range(NCORE)))

    K_all = np.concatenate([resA.results[c]["ko"] for c in range(NCORE)], axis=1)
    vparts = [resA.results[c]["vo"].reshape(128, NT, HD) for c in range(NCORE)]
    V_all = np.ascontiguousarray(
        np.stack(vparts, axis=1).reshape(128, NCORE * NT * HD))
    in_mapsB = []
    for c in range(NCORE):
        in_mapsB.append({
            "xin": resA.results[c]["xo"],
            "q": resA.results[c]["qo"],
            "kall": np.ascontiguousarray(K_all),
            "vall": V_all,
            "ident_bf": g["ident_bf"],
        })
    resB = run_bass_kernel_spmd(B.nc, in_mapsB, core_ids=list(range(NCORE)))
    out = np.stack([resB.results[c]["out"] for c in range(NCORE)], axis=0)
    return out
